# version 21
# baseline (speedup 1.0000x reference)
import numpy as np
import concourse.bass as bass
import concourse.mybir as mybir
from concourse.bass_utils import run_bass_kernel_spmd

F32 = mybir.dt.float32
BF16 = mybir.dt.bfloat16
AF = mybir.ActivationFunctionType
ALU = mybir.AluOpType

NCORES = 8
SEQ_PER_CORE = 4
S_LEN = 2048
D = 1024
TILE = 512
SUB = TILE // 128
NT = SEQ_PER_CORE * S_LEN // TILE
TPS = S_LEN // TILE
NCH = D // 128

ENGINES = ("tensor", "scalar", "vector", "gpsimd", "sync")


class Builder:

    DEFC = {"tensor": 1500.0, "scalar": 700.0, "vector": 600.0,
            "gpsimd": 5000.0, "sync": 5000.0}

    def __init__(self, nc):
        self.nc = nc
        self.ops = []
        self.sems = {}
        self.dma_sems = set()
        self.last_w = {}
        self.readers = {}

    def sem(self, name):
        if name not in self.sems:
            self.sems[name] = self.nc.alloc_semaphore(name=name)
        return self.sems[name]

    def op(self, eng, fn, reads=(), writes=(), waits=(), sem=None, dma=False,
           cost=None):
        amt = 16 if dma else 1
        if sem is None:
            sem = {"tensor": "sT", "scalar": "sS", "vector": "sV",
                   "gpsimd": "sG", "sync": "sIn"}[eng]
        if dma:
            self.dma_sems.add(sem)
        self.sem(sem)
        oid = len(self.ops)
        deps = set(w for w in waits if w is not None)
        for b in reads:
            w = self.last_w.get(b)
            if w is not None:
                deps.add(w)
        for b in writes:
            w = self.last_w.get(b)
            if w is not None:
                deps.add(w)
            deps.update(self.readers.get(b, ()))
        deps.discard(oid)
        self.ops.append({"eng": eng, "fn": fn, "deps": deps, "sem": sem,
                         "amt": amt, "cost": cost if cost is not None
                         else self.DEFC[eng]})
        for b in reads:
            self.readers.setdefault(b, []).append(oid)
        for b in writes:
            self.last_w[b] = oid
            self.readers[b] = []
        return oid

    def _schedule(self):
        import heapq
        n = len(self.ops)
        ndeps = [len(o["deps"]) for o in self.ops]
        dependents = [[] for _ in range(n)]
        for i, o in enumerate(self.ops):
            for d in o["deps"]:
                dependents[d].append(i)
        finish = [0.0] * n
        avail = {e: 0.0 for e in ENGINES}
        heaps = {e: [] for e in ENGINES}
        for i, o in enumerate(self.ops):
            if ndeps[i] == 0:
                heapq.heappush(heaps[o["eng"]], (0.0, i))
        order = {e: [] for e in ENGINES}
        nsched = 0
        while nsched < n:
            best = None
            for e in ENGINES:
                if heaps[e]:
                    r, i = heaps[e][0]
                    start = max(avail[e], r)
                    if best is None or (start, i) < (best[0], best[2]):
                        best = (start, e, i, r)
            start, e, i, r = best
            heapq.heappop(heaps[e])
            o = self.ops[i]
            f = start + o["cost"]
            finish[i] = f
            if e == "sync" or o["sem"] in self.dma_sems:
                avail[e] = start + 300.0
            else:
                avail[e] = f
            order[e].append(i)
            nsched += 1
            for j in dependents[i]:
                ndeps[j] -= 1
                if ndeps[j] == 0:
                    rt = max(finish[d] for d in self.ops[j]["deps"]) + 150.0
                    heapq.heappush(heaps[self.ops[j]["eng"]], (rt, j))
        return order

    def emit(self):
        order = self._schedule()
        cnt = {}
        tokval = [0] * len(self.ops)
        for e in ENGINES:
            for i in order[e]:
                s = self.ops[i]["sem"]
                cnt[s] = cnt.get(s, 0) + self.ops[i]["amt"]
                tokval[i] = cnt[s]
        builder = self
        own_sem = {"tensor": "sT", "scalar": "sS", "vector": "sV",
                   "gpsimd": "sG", "sync": None}
        with self.nc.Block() as block:
            def section(eng_name):
                def body(eng):
                    waited = {}
                    can_drain = eng_name in ("vector", "scalar", "gpsimd")
                    emitted = 0
                    drained_to = 0
                    osem = own_sem[eng_name]
                    for i in order[eng_name]:
                        o = builder.ops[i]
                        need_drain = False
                        for d in sorted(o["deps"]):
                            dop = builder.ops[d]
                            s, v = dop["sem"], tokval[d]
                            if dop["eng"] == eng_name and s not in builder.dma_sems:
                                if v > drained_to:
                                    need_drain = True
                                continue
                            if waited.get(s, 0) >= v:
                                continue
                            eng.wait_ge(builder.sems[s], v)
                            waited[s] = v
                        if need_drain and can_drain:
                            eng.drain()
                            drained_to = emitted
                        ins = o["fn"](eng)
                        emitted += o["amt"] if o["sem"] == osem else 0
                        if ins is not None:
                            ins.then_inc(builder.sems[o["sem"]], o["amt"])
                return body
            block.tensor(section("tensor"))
            block.scalar(section("scalar"))
            block.vector(section("vector"))
            block.gpsimd(section("gpsimd"))
            block.sync(section("sync"))


def build_program():
    nc = bass.Bass("TRN2", target_bir_lowering=False)
    B = Builder(nc)

    x_d = nc.dram_tensor("x", [SEQ_PER_CORE * S_LEN, D], F32, kind="ExternalInput").ap()
    wt_d = nc.dram_tensor("Wt", [D, D], F32, kind="ExternalInput").ap()
    a1_d = nc.dram_tensor("A1", [D, D], F32, kind="ExternalInput").ap()
    a2_d = nc.dram_tensor("A2", [D, D], F32, kind="ExternalInput").ap()
    wq_d = nc.dram_tensor("Wq", [D, D], F32, kind="ExternalInput").ap()
    wk_d = nc.dram_tensor("Wk", [D, D], F32, kind="ExternalInput").ap()
    wv_d = nc.dram_tensor("Wv", [D, D], F32, kind="ExternalInput").ap()
    wo_d = nc.dram_tensor("Wo", [D, D], F32, kind="ExternalInput").ap()
    bias_d = {n: nc.dram_tensor(n, [128, NCH], F32, kind="ExternalInput").ap()
              for n in ("bq", "bk", "bv", "bo", "bfn")}
    i32_d = nc.dram_tensor("I32", [128, 128], F32, kind="ExternalInput").ap()
    ibf_d = nc.dram_tensor("Ibf", [128, 128], BF16, kind="ExternalInput").ap()
    out_d = nc.dram_tensor("out", [SEQ_PER_CORE * S_LEN, D], F32, kind="ExternalOutput").ap()
    y1sp = nc.dram_tensor("y1sp", [NT, 128, NCH * TILE], F32, kind="Internal").ap()
    u2sp = nc.dram_tensor("u2sp", [NT, 128, NCH * TILE], BF16, kind="Internal").ap()

    sb = nc.alloc_sbuf_tensor
    Mb = sb("Mb", [128, NCH * D], BF16).ap()
    WA = sb("WA", [128, NCH * D], BF16).ap()
    WB = sb("WB", [128, NCH * D], BF16).ap()

    xy0 = sb("xy0", [128, 4096], F32).ap()
    xy1 = sb("xy1", [128, 4096], F32).ap()
    Y = sb("Y", [128, 4096], F32).ap()
    zk = sb("zk", [128, 4096], F32).ap()
    vv = sb("vv", [128, 4096], F32).ap()
    kk = sb("kk", [128, 4096], F32).ap()
    t1a = sb("t1a", [128, 4096], F32).ap()
    t1b = sb("t1b", [128, 4096], F32).ap()
    u_tm0 = sb("u_tm0", [128, 4096], BF16).ap()
    u_tm1 = u_tm0
    ufm0 = sb("ufm0", [128, 4096], BF16).ap()
    ufm1 = sb("ufm1", [128, 4096], BF16).ap()

    i32 = sb("i32", [128, 128], F32).ap()
    ibf = sb("ibf", [128, 128], BF16).ap()
    bias = {n: sb(n + "_s", [128, NCH], F32).ap() for n in ("bq", "bk", "bv", "bo", "bfn")}

    statbufs = [sb(f"statbuf{p}", [128, 7 * SUB], F32).ap() for p in range(2)]
    KsumP = sb("KsumP", [128, NCH * TPS], F32).ap()
    KVP = sb("KVP", [128, NCH * TPS], F32).ap()
    Ksumf = sb("Ksumf", [128, NCH * SEQ_PER_CORE], F32).ap()
    KVf = sb("KVf", [128, NCH * SEQ_PER_CORE], F32).ap()
    tk1 = sb("tk1", [128, NCH], F32).ap()
    cbuf = sb("cbuf", [128, 5], F32).ap()
    c_half = cbuf[:, 0:1]
    c_eps5 = cbuf[:, 1:2]
    c_eps6 = cbuf[:, 2:3]
    c_i1024 = cbuf[:, 3:4]
    c_ni1024 = cbuf[:, 4:5]

    psF = [nc.alloc_psum_tensor(f"psF{i}", [128, 512], F32).ap() for i in range(4)]
    psB = [nc.alloc_psum_tensor(f"psB{i}", [128, 1024], BF16).ap() for i in range(4)]
    psf_free = [None] * 4
    psb_free = [None] * 4

    B.op("vector", lambda v: (
        v.memset(c_half, 0.5), v.memset(c_eps5, 1e-5),
        v.memset(c_eps6, 1e-6), v.memset(c_i1024, 1.0 / D),
        v.memset(c_ni1024, -1.0 / D))[-1],
        writes=["consts"])
    B.op("sync", lambda s: s.dma_start(out=i32, in_=i32_d), writes=["i32"],
         sem="sIB0", dma=True)
    B.op("sync", lambda s: s.dma_start(out=ibf, in_=ibf_d), writes=["ibf"],
         sem="sIB1", dma=True)
    for bi_, n in enumerate(bias):
        B.op("sync", lambda s, a=bias[n], b=bias_d[n]: s.dma_start(out=a, in_=b),
             writes=["bias"], sem=f"sIB{2 + bi_}", dma=True)

    wl_n = [0]
    def wload(dst, src, buf):
        wl_n[0] += 1
        return B.op("gpsimd",
                    lambda g, a=dst, b=src: g.dma_start(
                        out=a.rearrange("p (c d) -> p c d", d=D),
                        in_=b.rearrange("(c p) d -> p c d", p=128)),
                    writes=[buf], sem=f"sW{wl_n[0]}", dma=True, cost=13000.0)

    wload(WA, wt_d, "WA")
    wload(WB, a1_d, "WB")

    def emit_product(lhs_buf, lhs_name, rhs_buf, rhs_name, out_buf, out_name):
        for mb in range(NCH):
            for nh in range(2):
                bi = (mb * 2 + nh) % 2
                bank = psF[bi]
                def mmfn(t, bank=bank, mb=mb, nh=nh, lhs=lhs_buf, rhs=rhs_buf):
                    ins = None
                    for k in range(NCH):
                        ins = t.matmul(out=bank,
                                       lhsT=lhs[:, D * k + 128 * mb: D * k + 128 * mb + 128],
                                       rhs=rhs[:, D * k + 512 * nh: D * k + 512 * nh + 512],
                                       start=(k == 0), stop=(k == NCH - 1))
                    return ins
                tokm = B.op("tensor", mmfn, reads=[lhs_name, rhs_name],
                            waits=[psf_free[bi]], cost=1900.0)
                tok_e = B.op("scalar",
                             lambda s, bank=bank, mb=mb, nh=nh, ob=out_buf: s.activation(
                                 out=ob[:, D * mb + 512 * nh: D * mb + 512 * nh + 512],
                                 in_=bank, func=AF.Copy),
                             writes=[out_name], waits=[tokm])
                psf_free[bi] = tok_e

    emit_product(WA, "WA", WB, "WB", Mb, "Mb")

    for ck in range(NCH):
        bi = ck % 4
        bank = psB[bi]
        def trfn(t, ck=ck, bank=bank):
            ins = None
            for rk in range(NCH):
                ins = t.transpose(out=bank[:, 128 * rk: 128 * rk + 128],
                                  in_=Mb[:, D * rk + 128 * ck: D * rk + 128 * ck + 128],
                                  identity=ibf)
            return ins
        tokt = B.op("tensor", trfn, reads=["Mb", "ibf"], waits=[psb_free[bi]])
        tok_e = B.op("scalar",
                     lambda s, ck=ck, bank=bank: s.activation(
                         out=WA[:, D * ck: D * ck + D], in_=bank, func=AF.Copy),
                     writes=["WA"], waits=[tokt])
        psb_free[bi] = tok_e

    wload(WB, a2_d, "WB")
    emit_product(WA, "WA", WB, "WB", Mb, "Mb")
    wload(WA, wk_d, "WA")
    wload(WB, wv_d, "WB")

    def projection(lhsT_buf, lhs_name, rhs_buf, rhs_name, evac_fn, evac_writes,
                   extra_reads=(), chunk_names=None):
        for mb in range(NCH):
            bi = mb % 4
            bank = psF[bi]
            def mmfn(t, mb=mb, bank=bank, lhs=lhsT_buf, rhs=rhs_buf):
                ins = None
                for k in range(NCH):
                    ins = t.matmul(out=bank,
                                   lhsT=lhs[:, D * k + 128 * mb: D * k + 128 * mb + 128],
                                   rhs=rhs[:, 512 * k: 512 * k + 512],
                                   start=(k == 0), stop=(k == NCH - 1))
                return ins
            tokm = B.op("tensor", mmfn, reads=[lhs_name, rhs_name],
                        waits=[psf_free[bi]], cost=1900.0)
            wl = [chunk_names[mb]] if chunk_names else list(evac_writes)
            tok_e = B.op("scalar", lambda s, mb=mb, bank=bank: evac_fn(s, mb, bank),
                         reads=list(extra_reads), writes=wl,
                         waits=[tokm], cost=700.0)
            psf_free[bi] = tok_e

    def tm_to_fm_f32(src_tm, src_name, dst_fm, dst_names):
        for c in range(NCH):
            bi = c % 4
            bank = psF[bi]
            def trfn(t, c=c, bank=bank):
                ins = None
                for s_ in range(SUB):
                    ins = t.transpose(
                        out=bank[:, 128 * s_: 128 * s_ + 128],
                        in_=src_tm[:, 1024 * s_ + 128 * c: 1024 * s_ + 128 * c + 128],
                        identity=i32)
                return ins
            tokt = B.op("tensor", trfn, reads=[src_name, "i32"], waits=[psf_free[bi]],
                        cost=700.0)
            tok_e = B.op("scalar",
                         lambda s, c=c, bank=bank: s.activation(
                             out=dst_fm[:, 512 * c: 512 * c + 512],
                             in_=bank, func=AF.Copy),
                         writes=[dst_names[c]], waits=[tokt], cost=700.0)
            psf_free[bi] = tok_e

    def tm_to_fm_bf(src_tm, src_name, dst_fm, dst_name):
        for bb in range(4):
            bank = psB[bb]
            def trfn(t, bb=bb, bank=bank):
                ins = None
                for ci in range(2):
                    c = 2 * bb + ci
                    for s_ in range(SUB):
                        ins = t.transpose(
                            out=bank[:, 512 * ci + 128 * s_: 512 * ci + 128 * s_ + 128],
                            in_=src_tm[:, 1024 * s_ + 128 * c: 1024 * s_ + 128 * c + 128],
                            identity=ibf)
                return ins
            tokt = B.op("tensor", trfn, reads=[src_name, "ibf"], waits=[psb_free[bb]],
                        cost=1300.0)
            tok_e = B.op("scalar",
                         lambda s, bb=bb, bank=bank: s.activation(
                             out=dst_fm[:, 1024 * bb: 1024 * bb + 1024],
                             in_=bank, func=AF.Copy),
                         writes=[dst_name], waits=[tokt], cost=1000.0)
            psb_free[bb] = tok_e

    def fm_to_tm(src_fm, src_names, dst_tm, dst_name, extra_reads=()):
        dst_cn = [f"{dst_name}%{b}" for b in range(8)]
        for b in range(8):
            s_ = b // 2
            ch0 = 4 * (b % 2)
            bi = b % 4
            bank = psF[bi]
            def trfn(t, s_=s_, ch0=ch0, bank=bank):
                ins = None
                for ci in range(4):
                    c = ch0 + ci
                    ins = t.transpose(
                        out=bank[:, 128 * ci: 128 * ci + 128],
                        in_=src_fm[:, 512 * c + 128 * s_: 512 * c + 128 * s_ + 128],
                        identity=i32)
                return ins
            tokt = B.op("tensor", trfn,
                        reads=[src_names[ch0 + ci] for ci in range(4)] + ["i32"]
                        + list(extra_reads),
                        waits=[psf_free[bi]], cost=700.0)
            tok_e = B.op("scalar",
                         lambda s, s_=s_, ch0=ch0, bank=bank: s.activation(
                             out=dst_tm[:, 1024 * s_ + 128 * ch0: 1024 * s_ + 128 * ch0 + 512],
                             in_=bank, func=AF.Copy),
                         writes=[dst_name, dst_cn[b]], waits=[tokt], cost=700.0)
            psf_free[bi] = tok_e

    def tree_reduce(src_ap, src_name, scratch, scratch_name, G, H, out_ap, out_name,
                    src_extra_reads=()):
        h = H
        cur, cur_name = src_ap, src_name
        def v3(ap, g, hh):
            return ap[:, 0:g * hh].rearrange("p (g h) -> p g h", h=hh)
        while h > 1:
            nh = h // 2
            dst, dst_name = (scratch, scratch_name) if h == H else (cur, cur_name)
            if nh == 1:
                dst_ap = out_ap.rearrange("p (g o) -> p g o", o=1)
                dn = out_name
            else:
                dst_ap = v3(dst, G, nh)
                dn = dst_name
            i0 = v3(cur, G, h)[:, :, 0:nh]
            i1 = v3(cur, G, h)[:, :, nh:h]
            extra = list(src_extra_reads) if h == H else []
            B.op("vector", lambda v, o=dst_ap, a=i0, b=i1: v.tensor_tensor(
                out=o, in0=a, in1=b, op=ALU.add),
                reads=[cur_name] + extra, writes=[dn],
                cost=(58.0 + G * nh) / 0.96 + 250.0)
            cur, cur_name = dst, dst_name
            h = nh

    def ln_stats(tm_buf, tm_name, sq_buf, sq_name, tr_buf, tr_name, par):
        st = statbufs[par]
        p = str(par)
        sx = st[:, 0:SUB]; sxx = st[:, SUB:2 * SUB]
        mean = st[:, 2 * SUB:3 * SUB]; istd = st[:, 3 * SUB:4 * SUB]
        tv1 = st[:, 4 * SUB:5 * SUB]; tv2 = st[:, 5 * SUB:6 * SUB]
        B.op("scalar", lambda s, o=sq_buf, i=tm_buf: s.activation(
            out=o, in_=i, func=AF.Square),
            reads=[tm_name],
            writes=[sq_name] + [f"{sq_name}@{j}" for j in range(NCH)], cost=3700.0)
        tree_reduce(tm_buf, tm_name, tr_buf, tr_name, SUB, 1024, sx, "sx" + p)
        tree_reduce(sq_buf, sq_name, sq_buf, sq_name, SUB, 1024, sxx, "sxx" + p,
                    src_extra_reads=[f"{sq_name}@{j}" for j in range(NCH)])
        B.op("vector", lambda v, sx=sx, mean=mean: v.tensor_scalar(
            out=mean, in0=sx, scalar1=c_i1024, scalar2=None, op0=ALU.mult),
            reads=["sx" + p, "consts"], writes=["mean" + p], cost=150.0)
        B.op("vector", lambda v, mean=mean, tv1=tv1: v.tensor_tensor(
            out=tv1, in0=mean, in1=mean, op=ALU.mult),
            reads=["mean" + p], writes=["tv1" + p], cost=150.0)
        B.op("vector", lambda v, sxx=sxx, tv1=tv1, tv2=tv2: v.scalar_tensor_tensor(
            out=tv2, in0=sxx, scalar=c_i1024, in1=tv1, op0=ALU.mult, op1=ALU.subtract),
            reads=["sxx" + p, "tv1" + p, "consts"], writes=["tv2" + p], cost=150.0)
        B.op("vector", lambda v, tv2=tv2: v.tensor_scalar(
            out=tv2, in0=tv2, scalar1=c_eps5, scalar2=None, op0=ALU.add),
            reads=["tv2" + p, "consts"], writes=["tv2" + p], cost=150.0)
        B.op("vector", lambda v, tv1=tv1, tv2=tv2: v.tensor_scalar(
            out=tv1, in0=tv2, scalar1=0.5, scalar2=0.5, op0=ALU.mult, op1=ALU.add),
            reads=["tv2" + p], writes=["tv1" + p], cost=150.0)
        for _ in range(2):
            B.op("vector", lambda v, tv1=tv1, istd=istd: v.reciprocal(out=istd, in_=tv1),
                 reads=["tv1" + p], writes=["istd" + p], cost=150.0)
            B.op("vector", lambda v, tv2=tv2, istd=istd: v.tensor_tensor(
                out=istd, in0=tv2, in1=istd, op=ALU.mult),
                reads=["tv2" + p, "istd" + p], writes=["istd" + p], cost=150.0)
            B.op("vector", lambda v, tv1=tv1, istd=istd: v.tensor_tensor(
                out=istd, in0=tv1, in1=istd, op=ALU.add),
                reads=["tv1" + p, "istd" + p], writes=["istd" + p], cost=150.0)
            B.op("vector", lambda v, istd=istd, tv1=tv1: v.tensor_scalar(
                out=tv1, in0=istd, scalar1=0.5, scalar2=None, op0=ALU.mult),
                reads=["istd" + p], writes=["tv1" + p], cost=150.0)
        B.op("vector", lambda v, tv1=tv1, istd=istd: v.reciprocal(out=istd, in_=tv1),
             reads=["tv1" + p], writes=["istd" + p], cost=150.0)

    def normalize(tm_buf, tm_name, u_tm, u_name, par):
        st = statbufs[par]
        p = str(par)
        mean = st[:, 2 * SUB:3 * SUB]; istd = st[:, 3 * SUB:4 * SUB]
        for s_ in range(SUB):
            B.op("vector", lambda v, s_=s_, u=u_tm, i=tm_buf, mean=mean, istd=istd: v.tensor_scalar(
                out=u[:, 1024 * s_: 1024 * s_ + 1024],
                in0=i[:, 1024 * s_: 1024 * s_ + 1024],
                scalar1=mean[:, s_: s_ + 1], scalar2=istd[:, s_: s_ + 1],
                op0=ALU.subtract, op1=ALU.mult),
                reads=[tm_name, "mean" + p, "istd" + p], writes=[u_name], cost=1150.0)

    def cn(base):
        return [f"{base}@{j}" for j in range(NCH)]

    ufms = [(ufm0, "ufm0"), (ufm1, "ufm1")]
    utms = [(u_tm0, "u_tm0"), (u_tm1, "u_tm0")]
    t1s = [(t1a, "t1a"), (t1b, "t1b")]
    xys = [(xy0, "xy0"), (xy1, "xy1")]

    def p1_A(idx):
        xy, xyn = xys[idx % 2]
        t1, t1n = t1s[idx % 2]
        B.op("sync", lambda s, idx=idx, xy=xy: s.dma_start(
            out=xy.rearrange("p (sb d) -> p sb d", d=D),
            in_=x_d[idx * TILE:(idx + 1) * TILE, :].rearrange("(sb p) d -> p sb d", p=128)),
            writes=[xyn], sem=f"sIX{idx % 2}", dma=True, cost=7000.0)
        tm_to_fm_f32(xy, xyn, Y, cn("Y"))
        ln_stats(xy, xyn, t1, t1n, kk, "kk", idx % 2)
        normalize(xy, xyn, *utms[idx % 2], idx % 2)

    def p1_B(idx):
        ufm, ufm_name = ufms[idx % 2]
        u_tm, u_name = utms[idx % 2]
        t1, t1n = t1s[idx % 2]
        tm_to_fm_bf(u_tm, u_name, ufm, ufm_name)
        projection(Mb, "Mb", ufm, ufm_name,
                   lambda s, mb, bank, t1=t1: s.activation(
                       out=t1[:, 512 * mb: 512 * mb + 512], in_=bank,
                       func=AF.Tanh, bias=bias["bfn"][:, mb: mb + 1], scale=1.0),
                   [t1n], extra_reads=["bias"], chunk_names=cn(t1n))
        for mb in range(NCH):
            B.op("vector", lambda v, t1=t1, mb=mb: v.scalar_tensor_tensor(
                out=Y[:, 512 * mb: 512 * mb + 512],
                in0=t1[:, 512 * mb: 512 * mb + 512], scalar=c_half,
                in1=Y[:, 512 * mb: 512 * mb + 512], op0=ALU.mult, op1=ALU.add),
                reads=[cn(t1n)[mb], f"Y@{mb}", "consts"], writes=[f"Y@{mb}"], cost=600.0)
        B.op("sync", lambda s, idx=idx: s.dma_start(out=y1sp[idx], in_=Y),
             reads=cn("Y"), sem="sSY", dma=True, cost=7000.0)
        fm_to_tm(Y, cn("Y"), vv, "vv")
        ln_stats(vv, "vv", t1, t1n, kk, "kk", idx % 2)
        normalize(vv, "vv", u_tm, u_name, idx % 2)
        tm_to_fm_bf(u_tm, u_name, ufm, ufm_name)
        B.op("sync", lambda s, idx=idx, u=ufm: s.dma_start(out=u2sp[idx], in_=u),
             reads=[ufm_name], sem=f"sSU{idx % 2}", dma=True, cost=4000.0)

    def p1_C(idx):
        ufm, ufm_name = ufms[idx % 2]
        projection(WA, "WA", ufm, ufm_name,
                   lambda s, mb, bank: s.activation(
                       out=zk[:, 512 * mb: 512 * mb + 512], in_=bank,
                       func=AF.Identity, bias=bias["bk"][:, mb: mb + 1], scale=1.0),
                   ["zk"], extra_reads=["bias"])
        projection(WB, "WB", ufm, ufm_name,
                   lambda s, mb, bank: s.activation(
                       out=vv[:, 512 * mb: 512 * mb + 512], in_=bank, func=AF.Copy),
                   ["vv"])

    def p1_D(idx):
        jt = idx % TPS
        seq = idx // TPS
        B.op("vector", lambda v: v.tensor_scalar(
            out=kk, in0=zk, scalar1=0.0, scalar2=None, op0=ALU.min),
            reads=["zk"], writes=["kk"], cost=2200.0)
        B.op("scalar", lambda s: s.activation(out=kk, in_=kk, func=AF.Exp),
             reads=["kk"], writes=["kk"], cost=3700.0)
        B.op("vector", lambda v: v.scalar_tensor_tensor(
            out=kk, in0=zk, scalar=0.0, in1=kk, op0=ALU.max, op1=ALU.add),
            reads=["zk", "kk"], writes=["kk"], cost=4300.0)
        B.op("vector", lambda v: v.scalar_tensor_tensor(
            out=zk, in0=kk, scalar=1.0, in1=vv, op0=ALU.mult, op1=ALU.mult),
            reads=["kk", "vv"], writes=["zk"], cost=4300.0)
        tree_reduce(kk, "kk", kk, "kk", NCH, 512,
                    KsumP[:, NCH * jt: NCH * jt + NCH], "KsumP")
        tree_reduce(zk, "zk", zk, "zk", NCH, 512,
                    KVP[:, NCH * jt: NCH * jt + NCH], "KVP")
        if jt == TPS - 1:
            def comb(v, dst, src):
                v.tensor_tensor(out=tk1, in0=src[:, 0:NCH], in1=src[:, NCH:2 * NCH], op=ALU.add)
                v.drain()
                v.tensor_tensor(out=tk1, in0=tk1, in1=src[:, 2 * NCH:3 * NCH], op=ALU.add)
                v.drain()
                return v.tensor_tensor(out=dst[:, NCH * seq: NCH * seq + NCH],
                                       in0=tk1, in1=src[:, 3 * NCH:4 * NCH], op=ALU.add)
            B.op("vector", lambda v, f=comb: f(v, Ksumf, KsumP),
                 reads=["KsumP"], writes=["Ksumf", "tk1"])
            B.op("vector", lambda v, f=comb: f(v, KVf, KVP),
                 reads=["KVP"], writes=["KVf", "tk1"])
            B.op("vector", lambda v, seq=seq: v.tensor_tensor(
                out=tk1, in0=bias["bv"], in1=Ksumf[:, NCH * seq: NCH * seq + NCH],
                op=ALU.mult),
                reads=["bias", "Ksumf"], writes=["tk1"])
            B.op("vector", lambda v, seq=seq: v.tensor_tensor(
                out=KVf[:, NCH * seq: NCH * seq + NCH],
                in0=KVf[:, NCH * seq: NCH * seq + NCH], in1=tk1, op=ALU.add),
                reads=["tk1", "KVf"], writes=["KVf"])

    p1_A(0)
    p1_B(0)
    for k in range(1, NT):
        p1_A(k)
        p1_C(k - 1)
        p1_D(k - 1)
        p1_B(k)
    p1_C(NT - 1)
    p1_D(NT - 1)

    wload(WA, wq_d, "WA")
    wload(WB, wo_d, "WB")
    tok_all_spills_list = [i for i, o in enumerate(B.ops)
                           if o["sem"] in ("sSY", "sSU0", "sSU1")]

    def p2_load_u2(idx, extra=()):
        ufm, ufm_name = ufms[idx % 2]
        B.op("sync", lambda s, idx=idx, u=ufm: s.dma_start(out=u, in_=u2sp[idx]),
             writes=[ufm_name], waits=list(extra), sem=f"sLU{idx % 2}", dma=True, cost=4000.0)

    def p2_load_y1(idx, extra=()):
        B.op("sync", lambda s, idx=idx: s.dma_start(out=Y, in_=y1sp[idx]),
             writes=cn("Y"), waits=list(extra), sem="sLY", dma=True, cost=7000.0)

    def p2_Q(idx):
        ufm, ufm_name = ufms[idx % 2]
        projection(WA, "WA", ufm, ufm_name,
                   lambda s, mb, bank: s.activation(
                       out=zk[:, 512 * mb: 512 * mb + 512], in_=bank,
                       func=AF.Identity, bias=bias["bq"][:, mb: mb + 1], scale=1.0),
                   ["zk"], extra_reads=["bias"])

    def p2_R(idx):
        seq = idx // TPS
        t1, t1n = t1s[idx % 2]
        B.op("vector", lambda v: v.tensor_scalar(
            out=kk, in0=zk, scalar1=0.0, scalar2=None, op0=ALU.min),
            reads=["zk"], writes=["kk"], cost=2200.0)
        B.op("scalar", lambda s: s.activation(out=kk, in_=kk, func=AF.Exp),
             reads=["kk"], writes=["kk"], cost=3700.0)
        B.op("vector", lambda v: v.scalar_tensor_tensor(
            out=kk, in0=zk, scalar=0.0, in1=kk, op0=ALU.max, op1=ALU.add),
            reads=["zk", "kk"], writes=["kk"], cost=4300.0)
        for c in range(NCH):
            B.op("vector", lambda v, c=c, seq=seq: v.tensor_scalar(
                out=vv[:, 512 * c: 512 * c + 512],
                in0=kk[:, 512 * c: 512 * c + 512],
                scalar1=Ksumf[:, NCH * seq + c: NCH * seq + c + 1],
                scalar2=c_eps6, op0=ALU.mult, op1=ALU.add),
                reads=["kk", "Ksumf", "consts"], writes=["vv"], cost=600.0)
        B.op("scalar", lambda s, t1=t1: s.activation(out=t1, in_=vv, func=AF.Ln),
             reads=["vv"], writes=[t1n] + cn(t1n), cost=3700.0)
        for c in range(NCH):
            B.op("vector", lambda v, c=c, seq=seq: v.tensor_scalar(
                out=kk[:, 512 * c: 512 * c + 512],
                in0=kk[:, 512 * c: 512 * c + 512],
                scalar1=KVf[:, NCH * seq + c: NCH * seq + c + 1],
                scalar2=None, op0=ALU.mult),
                reads=["kk", "KVf"], writes=["kk"], cost=600.0)
        B.op("scalar", lambda s, t1=t1: s.activation(out=t1, in_=t1, func=AF.Exp, scale=-1.0),
             reads=[t1n], writes=[t1n] + cn(t1n), cost=3700.0)
        u_tm, u_name = utms[idx % 2]
        B.op("vector", lambda v, u=u_tm, t1=t1: v.tensor_tensor(
            out=u, in0=kk, in1=t1, op=ALU.mult),
            reads=["kk", t1n] + cn(t1n), writes=[u_name], cost=4300.0)

    def p2_W(idx):
        u_tm, u_name = utms[idx % 2]
        t1, t1n = t1s[idx % 2]
        projection(WB, "WB", u_tm, u_name,
                   lambda s, mb, bank, t1=t1: s.activation(
                       out=t1[:, 512 * mb: 512 * mb + 512], in_=bank,
                       func=AF.Gelu_apprx_tanh, bias=bias["bo"][:, mb: mb + 1], scale=1.0),
                   [t1n], extra_reads=["bias"], chunk_names=cn(t1n))
        for mb in range(NCH):
            B.op("vector", lambda v, t1=t1, mb=mb: v.tensor_tensor(
                out=Y[:, 512 * mb: 512 * mb + 512],
                in0=Y[:, 512 * mb: 512 * mb + 512],
                in1=t1[:, 512 * mb: 512 * mb + 512], op=ALU.add),
                reads=[f"Y@{mb}", cn(t1n)[mb]], writes=[f"Y@{mb}"], cost=600.0)

    def p2_L3(idx):
        xy, xyn = xys[idx % 2]
        t1, t1n = t1s[idx % 2]
        fm_to_tm(Y, cn("Y"), xy, xyn)
        ln_stats(xy, xyn, t1, t1n, vv, "vv", idx % 2)
        normalize(xy, xyn, *utms[idx % 2], idx % 2)

    def p2_F(idx):
        ufm, ufm_name = ufms[idx % 2]
        u_tm, u_name = utms[idx % 2]
        t1, t1n = t1s[idx % 2]
        xy, xyn = xys[idx % 2]
        tm_to_fm_bf(u_tm, u_name, ufm, ufm_name)
        projection(Mb, "Mb", ufm, ufm_name,
                   lambda s, mb, bank, t1=t1: s.activation(
                       out=t1[:, 512 * mb: 512 * mb + 512], in_=bank,
                       func=AF.Tanh, bias=bias["bfn"][:, mb: mb + 1], scale=1.0),
                   [t1n], extra_reads=["bias"], chunk_names=cn(t1n))
        for mb in range(NCH):
            B.op("vector", lambda v, t1=t1, mb=mb: v.scalar_tensor_tensor(
                out=vv[:, 512 * mb: 512 * mb + 512],
                in0=t1[:, 512 * mb: 512 * mb + 512], scalar=c_half,
                in1=Y[:, 512 * mb: 512 * mb + 512], op0=ALU.mult, op1=ALU.add),
                reads=[cn(t1n)[mb], f"Y@{mb}", "consts", "vv"], writes=[cn("vvo")[mb]],
                cost=600.0)
        fm_to_tm(vv, cn("vvo"), xy, xyn, extra_reads=["vv"])
        B.op("sync", lambda s, idx=idx, xy=xy: s.dma_start(
            out=out_d[idx * TILE:(idx + 1) * TILE, :].rearrange("(sb p) d -> p sb d", p=128),
            in_=xy.rearrange("p (sb d) -> p sb d", d=D)),
            reads=[xyn], sem=f"sOU{idx % 2}", dma=True, cost=7000.0)

    p2_load_u2(0, extra=tok_all_spills_list)
    p2_load_y1(0, extra=tok_all_spills_list)
    p2_Q(0)
    for k in range(NT):
        if k > 0:
            p2_F(k - 1)
            p2_load_y1(k)
        if k + 1 < NT:
            p2_load_u2(k + 1)
        p2_R(k)
        p2_W(k)
        p2_L3(k)
        if k + 1 < NT:
            p2_Q(k + 1)
    p2_F(NT - 1)

    all_stores = [i for i, o in enumerate(B.ops)
                  if o["sem"] in ("sOU0", "sOU1", "sSY", "sSU0", "sSU1")]
    B.op("sync", lambda s: s.nop(), waits=all_stores, cost=10.0)
    B.emit()
    return nc


def kernel(L0_attn_x, L0_attn_Wq, L0_attn_bq, L0_attn_Wk, L0_attn_bk,
           L0_attn_Wv, L0_attn_bv, L0_attn_Wo, L0_attn_bo,
           L0_ffn_weight, L0_ffn_aux_weight1, L0_ffn_aux_weight2, L0_ffn_bias):
    import ml_dtypes
    x = np.ascontiguousarray(np.asarray(L0_attn_x, dtype=np.float32))
    Wq = np.ascontiguousarray(np.asarray(L0_attn_Wq, np.float32))
    Wk = np.ascontiguousarray(np.asarray(L0_attn_Wk, np.float32))
    Wv = np.ascontiguousarray(np.asarray(L0_attn_Wv, np.float32))
    Wo = np.ascontiguousarray(np.asarray(L0_attn_Wo, np.float32))
    W = np.asarray(L0_ffn_weight, np.float32)
    A1 = np.ascontiguousarray(np.asarray(L0_ffn_aux_weight1, np.float32))
    A2 = np.ascontiguousarray(np.asarray(L0_ffn_aux_weight2, np.float32))
    Wt = np.ascontiguousarray(W.T)

    def bias_fm(b):
        return np.ascontiguousarray(np.asarray(b, np.float32).reshape(NCH, 128).T)

    ident = np.eye(128, dtype=np.float32)
    identbf = ident.astype(ml_dtypes.bfloat16)

    nc = build_program()
    in_maps = []
    for ci in range(NCORES):
        xs = x[ci * SEQ_PER_CORE:(ci + 1) * SEQ_PER_CORE].reshape(SEQ_PER_CORE * S_LEN, D)
        in_maps.append({
            "x": np.ascontiguousarray(xs),
            "Wt": Wt, "A1": A1, "A2": A2,
            "Wq": Wq, "Wk": Wk, "Wv": Wv, "Wo": Wo,
            "bq": bias_fm(L0_attn_bq), "bk": bias_fm(L0_attn_bk),
            "bv": bias_fm(L0_attn_bv), "bo": bias_fm(L0_attn_bo),
            "bfn": bias_fm(L0_ffn_bias),
            "I32": ident, "Ibf": identbf,
        })
    res = run_bass_kernel_spmd(nc, in_maps, core_ids=list(range(NCORES)))
    out = np.concatenate([r["out"].reshape(SEQ_PER_CORE, S_LEN, D)
                          for r in res.results], axis=0)
    return out


# version 22
# speedup vs baseline: 1.2230x; 1.2230x over previous
import numpy as np
import concourse.bass as bass
import concourse.mybir as mybir
from concourse.bass_utils import run_bass_kernel_spmd

F32 = mybir.dt.float32
BF16 = mybir.dt.bfloat16
AF = mybir.ActivationFunctionType
ALU = mybir.AluOpType

NCORES = 8
SEQ_PER_CORE = 4
S_LEN = 2048
D = 1024
TILE = 512
SUB = TILE // 128
NT = SEQ_PER_CORE * S_LEN // TILE
TPS = S_LEN // TILE
NCH = D // 128

ENGINES = ("tensor", "scalar", "vector", "gpsimd", "sync")


class Builder:

    DEFC = {"tensor": 1500.0, "scalar": 700.0, "vector": 600.0,
            "gpsimd": 5000.0, "sync": 5000.0}

    def __init__(self, nc):
        self.nc = nc
        self.ops = []
        self.sems = {}
        self.dma_sems = set()
        self.last_w = {}
        self.readers = {}

    def sem(self, name):
        if name not in self.sems:
            self.sems[name] = self.nc.alloc_semaphore(name=name)
        return self.sems[name]

    def op(self, eng, fn, reads=(), writes=(), waits=(), sem=None, dma=False,
           cost=None):
        amt = 16 if dma else 1
        if sem is None:
            sem = {"tensor": "sT", "scalar": "sS", "vector": "sV",
                   "gpsimd": "sG", "sync": "sIn"}[eng]
        if dma:
            self.dma_sems.add(sem)
        self.sem(sem)
        oid = len(self.ops)
        deps = set(w for w in waits if w is not None)
        for b in reads:
            w = self.last_w.get(b)
            if w is not None:
                deps.add(w)
        for b in writes:
            w = self.last_w.get(b)
            if w is not None:
                deps.add(w)
            deps.update(self.readers.get(b, ()))
        deps.discard(oid)
        self.ops.append({"eng": eng, "fn": fn, "deps": deps, "sem": sem,
                         "amt": amt, "cost": cost if cost is not None
                         else self.DEFC[eng]})
        for b in reads:
            self.readers.setdefault(b, []).append(oid)
        for b in writes:
            self.last_w[b] = oid
            self.readers[b] = []
        return oid

    def _schedule(self):
        import heapq
        n = len(self.ops)
        ndeps = [len(o["deps"]) for o in self.ops]
        dependents = [[] for _ in range(n)]
        for i, o in enumerate(self.ops):
            for d in o["deps"]:
                dependents[d].append(i)
        finish = [0.0] * n
        avail = {e: 0.0 for e in ENGINES}
        heaps = {e: [] for e in ENGINES}
        for i, o in enumerate(self.ops):
            if ndeps[i] == 0:
                heapq.heappush(heaps[o["eng"]], (0.0, i))
        order = {e: [] for e in ENGINES}
        nsched = 0
        while nsched < n:
            best = None
            for e in ENGINES:
                if heaps[e]:
                    r, i = heaps[e][0]
                    start = max(avail[e], r)
                    if best is None or (start, i) < (best[0], best[2]):
                        best = (start, e, i, r)
            start, e, i, r = best
            heapq.heappop(heaps[e])
            o = self.ops[i]
            f = start + o["cost"]
            finish[i] = f
            if e == "sync" or o["sem"] in self.dma_sems:
                avail[e] = start + 300.0
            else:
                avail[e] = f
            order[e].append(i)
            nsched += 1
            for j in dependents[i]:
                ndeps[j] -= 1
                if ndeps[j] == 0:
                    rt = max(finish[d] for d in self.ops[j]["deps"]) + 150.0
                    heapq.heappush(heaps[self.ops[j]["eng"]], (rt, j))
        return order

    def emit(self):
        order = self._schedule()
        cnt = {}
        tokval = [0] * len(self.ops)
        for e in ENGINES:
            for i in order[e]:
                s = self.ops[i]["sem"]
                cnt[s] = cnt.get(s, 0) + self.ops[i]["amt"]
                tokval[i] = cnt[s]
        builder = self
        own_sem = {"tensor": "sT", "scalar": "sS", "vector": "sV",
                   "gpsimd": "sG", "sync": None}
        with self.nc.Block() as block:
            def section(eng_name):
                def body(eng):
                    waited = {}
                    can_drain = eng_name in ("vector", "scalar", "gpsimd")
                    emitted = 0
                    drained_to = 0
                    osem = own_sem[eng_name]
                    for i in order[eng_name]:
                        o = builder.ops[i]
                        need_drain = False
                        for d in sorted(o["deps"]):
                            dop = builder.ops[d]
                            s, v = dop["sem"], tokval[d]
                            if dop["eng"] == eng_name and s not in builder.dma_sems:
                                if v > drained_to:
                                    need_drain = True
                                continue
                            if waited.get(s, 0) >= v:
                                continue
                            eng.wait_ge(builder.sems[s], v)
                            waited[s] = v
                        if need_drain and can_drain:
                            eng.drain()
                            drained_to = emitted
                        ins = o["fn"](eng)
                        emitted += o["amt"] if o["sem"] == osem else 0
                        if ins is not None:
                            ins.then_inc(builder.sems[o["sem"]], o["amt"])
                return body
            block.tensor(section("tensor"))
            block.scalar(section("scalar"))
            block.vector(section("vector"))
            block.gpsimd(section("gpsimd"))
            block.sync(section("sync"))


def build_program():
    nc = bass.Bass("TRN2", target_bir_lowering=False)
    B = Builder(nc)

    x_d = nc.dram_tensor("x", [SEQ_PER_CORE * S_LEN, D], F32, kind="ExternalInput").ap()
    wt_d = nc.dram_tensor("Wt", [D, D], F32, kind="ExternalInput").ap()
    a1_d = nc.dram_tensor("A1", [D, D], F32, kind="ExternalInput").ap()
    a2_d = nc.dram_tensor("A2", [D, D], F32, kind="ExternalInput").ap()
    wq_d = nc.dram_tensor("Wq", [D, D], F32, kind="ExternalInput").ap()
    wk_d = nc.dram_tensor("Wk", [D, D], F32, kind="ExternalInput").ap()
    wv_d = nc.dram_tensor("Wv", [D, D], F32, kind="ExternalInput").ap()
    wo_d = nc.dram_tensor("Wo", [D, D], F32, kind="ExternalInput").ap()
    bias_d = {n: nc.dram_tensor(n, [128, NCH], F32, kind="ExternalInput").ap()
              for n in ("bq", "bk", "bv", "bo", "bfn")}
    i32_d = nc.dram_tensor("I32", [128, 128], F32, kind="ExternalInput").ap()
    ibf_d = nc.dram_tensor("Ibf", [128, 128], BF16, kind="ExternalInput").ap()
    out_d = nc.dram_tensor("out", [SEQ_PER_CORE * S_LEN, D], F32, kind="ExternalOutput").ap()
    y1sp = nc.dram_tensor("y1sp", [NT, 128, NCH * TILE], F32, kind="Internal").ap()
    u2sp = nc.dram_tensor("u2sp", [NT, 128, NCH * TILE], BF16, kind="Internal").ap()

    sb = nc.alloc_sbuf_tensor
    Mb = sb("Mb", [128, NCH * D], BF16).ap()
    WA = sb("WA", [128, NCH * D], BF16).ap()
    WB = sb("WB", [128, NCH * D], BF16).ap()

    xy0 = sb("xy0", [128, 4096], F32).ap()
    xy1 = sb("xy1", [128, 4096], F32).ap()
    Y = sb("Y", [128, 4096], F32).ap()
    zk = sb("zk", [128, 4096], F32).ap()
    vv = sb("vv", [128, 4096], F32).ap()
    kk = sb("kk", [128, 4096], F32).ap()
    t1a = sb("t1a", [128, 4096], F32).ap()
    t1b = sb("t1b", [128, 4096], F32).ap()
    u_tm0 = sb("u_tm0", [128, 4096], BF16).ap()
    u_tm1 = u_tm0
    ufm0 = sb("ufm0", [128, 4096], BF16).ap()
    ufm1 = sb("ufm1", [128, 4096], BF16).ap()

    i32 = sb("i32", [128, 128], F32).ap()
    ibf = sb("ibf", [128, 128], BF16).ap()
    bias = {n: sb(n + "_s", [128, NCH], F32).ap() for n in ("bq", "bk", "bv", "bo", "bfn")}

    statbufs = [sb(f"statbuf{p}", [128, 7 * SUB], F32).ap() for p in range(2)]
    KsumP = sb("KsumP", [128, NCH * TPS], F32).ap()
    KVP = sb("KVP", [128, NCH * TPS], F32).ap()
    Ksumf = sb("Ksumf", [128, NCH * SEQ_PER_CORE], F32).ap()
    KVf = sb("KVf", [128, NCH * SEQ_PER_CORE], F32).ap()
    tk1 = sb("tk1", [128, NCH], F32).ap()
    cbuf = sb("cbuf", [128, 5], F32).ap()
    c_half = cbuf[:, 0:1]
    c_eps5 = cbuf[:, 1:2]
    c_eps6 = cbuf[:, 2:3]
    c_i1024 = cbuf[:, 3:4]
    c_ni1024 = cbuf[:, 4:5]

    psF = [nc.alloc_psum_tensor(f"psF{i}", [128, 512], F32).ap() for i in range(6)]
    psB = [nc.alloc_psum_tensor(f"psB{i}", [128, 1024], BF16).ap() for i in range(2)]
    psf_free = [None] * 6
    psb_free = [None] * 2

    B.op("vector", lambda v: (
        v.memset(c_half, 0.5), v.memset(c_eps5, 1e-5),
        v.memset(c_eps6, 1e-6), v.memset(c_i1024, 1.0 / D),
        v.memset(c_ni1024, -1.0 / D))[-1],
        writes=["consts"])
    B.op("sync", lambda s: s.dma_start(out=i32, in_=i32_d), writes=["i32"],
         sem="sIB0", dma=True)
    B.op("sync", lambda s: s.dma_start(out=ibf, in_=ibf_d), writes=["ibf"],
         sem="sIB1", dma=True)
    for bi_, n in enumerate(bias):
        B.op("sync", lambda s, a=bias[n], b=bias_d[n]: s.dma_start(out=a, in_=b),
             writes=["bias"], sem=f"sIB{2 + bi_}", dma=True)

    wl_n = [0]
    def wload(dst, src, buf):
        wl_n[0] += 1
        return B.op("gpsimd",
                    lambda g, a=dst, b=src: g.dma_start(
                        out=a.rearrange("p (c d) -> p c d", d=D),
                        in_=b.rearrange("(c p) d -> p c d", p=128)),
                    writes=[buf], sem=f"sW{wl_n[0]}", dma=True, cost=13000.0)

    wload(WA, wt_d, "WA")
    wload(WB, a1_d, "WB")

    def emit_product(lhs_buf, lhs_name, rhs_buf, rhs_name, out_buf, out_name):
        for mb in range(NCH):
            for nh in range(2):
                bi = (mb * 2 + nh) % 6
                bank = psF[bi]
                def mmfn(t, bank=bank, mb=mb, nh=nh, lhs=lhs_buf, rhs=rhs_buf):
                    ins = None
                    for k in range(NCH):
                        ins = t.matmul(out=bank,
                                       lhsT=lhs[:, D * k + 128 * mb: D * k + 128 * mb + 128],
                                       rhs=rhs[:, D * k + 512 * nh: D * k + 512 * nh + 512],
                                       start=(k == 0), stop=(k == NCH - 1))
                    return ins
                tokm = B.op("tensor", mmfn, reads=[lhs_name, rhs_name],
                            waits=[psf_free[bi]], cost=1900.0)
                tok_e = B.op("scalar",
                             lambda s, bank=bank, mb=mb, nh=nh, ob=out_buf: s.activation(
                                 out=ob[:, D * mb + 512 * nh: D * mb + 512 * nh + 512],
                                 in_=bank, func=AF.Copy),
                             writes=[out_name], waits=[tokm])
                psf_free[bi] = tok_e

    emit_product(WA, "WA", WB, "WB", Mb, "Mb")

    for ck in range(NCH):
        bi = ck % 2
        bank = psB[bi]
        def trfn(t, ck=ck, bank=bank):
            ins = None
            for rk in range(NCH):
                ins = t.transpose(out=bank[:, 128 * rk: 128 * rk + 128],
                                  in_=Mb[:, D * rk + 128 * ck: D * rk + 128 * ck + 128],
                                  identity=ibf)
            return ins
        tokt = B.op("tensor", trfn, reads=["Mb", "ibf"], waits=[psb_free[bi]])
        tok_e = B.op("scalar",
                     lambda s, ck=ck, bank=bank: s.activation(
                         out=WA[:, D * ck: D * ck + D], in_=bank, func=AF.Copy),
                     writes=["WA"], waits=[tokt])
        psb_free[bi] = tok_e

    wload(WB, a2_d, "WB")
    emit_product(WA, "WA", WB, "WB", Mb, "Mb")
    wload(WA, wk_d, "WA")
    wload(WB, wv_d, "WB")

    def projection(lhsT_buf, lhs_name, rhs_buf, rhs_name, evac_fn, evac_writes,
                   extra_reads=(), chunk_names=None):
        for mb in range(NCH):
            bi = mb % 6
            bank = psF[bi]
            def mmfn(t, mb=mb, bank=bank, lhs=lhsT_buf, rhs=rhs_buf):
                ins = None
                for k in range(NCH):
                    ins = t.matmul(out=bank,
                                   lhsT=lhs[:, D * k + 128 * mb: D * k + 128 * mb + 128],
                                   rhs=rhs[:, 512 * k: 512 * k + 512],
                                   start=(k == 0), stop=(k == NCH - 1))
                return ins
            tokm = B.op("tensor", mmfn, reads=[lhs_name, rhs_name],
                        waits=[psf_free[bi]], cost=1900.0)
            wl = [chunk_names[mb]] if chunk_names else list(evac_writes)
            tok_e = B.op("scalar", lambda s, mb=mb, bank=bank: evac_fn(s, mb, bank),
                         reads=list(extra_reads), writes=wl,
                         waits=[tokm], cost=700.0)
            psf_free[bi] = tok_e

    def tm_to_fm_f32(src_tm, src_name, dst_fm, dst_names):
        for c in range(NCH):
            bi = c % 6
            bank = psF[bi]
            def trfn(t, c=c, bank=bank):
                ins = None
                for s_ in range(SUB):
                    ins = t.transpose(
                        out=bank[:, 128 * s_: 128 * s_ + 128],
                        in_=src_tm[:, 1024 * s_ + 128 * c: 1024 * s_ + 128 * c + 128],
                        identity=i32)
                return ins
            tokt = B.op("tensor", trfn, reads=[src_name, "i32"], waits=[psf_free[bi]],
                        cost=700.0)
            tok_e = B.op("scalar",
                         lambda s, c=c, bank=bank: s.activation(
                             out=dst_fm[:, 512 * c: 512 * c + 512],
                             in_=bank, func=AF.Copy),
                         writes=[dst_names[c]], waits=[tokt], cost=700.0)
            psf_free[bi] = tok_e

    def tm_to_fm_bf(src_tm, src_name, dst_fm, dst_name):
        for bb in range(4):
            bank = psB[bb % 2]
            def trfn(t, bb=bb, bank=bank):
                ins = None
                for ci in range(2):
                    c = 2 * bb + ci
                    for s_ in range(SUB):
                        ins = t.transpose(
                            out=bank[:, 512 * ci + 128 * s_: 512 * ci + 128 * s_ + 128],
                            in_=src_tm[:, 1024 * s_ + 128 * c: 1024 * s_ + 128 * c + 128],
                            identity=ibf)
                return ins
            tokt = B.op("tensor", trfn, reads=[src_name, "ibf"], waits=[psb_free[bb % 2]],
                        cost=1300.0)
            tok_e = B.op("scalar",
                         lambda s, bb=bb, bank=bank: s.activation(
                             out=dst_fm[:, 1024 * bb: 1024 * bb + 1024],
                             in_=bank, func=AF.Copy),
                         writes=[dst_name], waits=[tokt], cost=1000.0)
            psb_free[bb % 2] = tok_e

    def fm_to_tm(src_fm, src_names, dst_tm, dst_name, extra_reads=()):
        dst_cn = [f"{dst_name}%{b}" for b in range(8)]
        for b in range(8):
            s_ = b // 2
            ch0 = 4 * (b % 2)
            bi = b % 6
            bank = psF[bi]
            def trfn(t, s_=s_, ch0=ch0, bank=bank):
                ins = None
                for ci in range(4):
                    c = ch0 + ci
                    ins = t.transpose(
                        out=bank[:, 128 * ci: 128 * ci + 128],
                        in_=src_fm[:, 512 * c + 128 * s_: 512 * c + 128 * s_ + 128],
                        identity=i32)
                return ins
            tokt = B.op("tensor", trfn,
                        reads=[src_names[ch0 + ci] for ci in range(4)] + ["i32"]
                        + list(extra_reads),
                        waits=[psf_free[bi]], cost=700.0)
            tok_e = B.op("scalar",
                         lambda s, s_=s_, ch0=ch0, bank=bank: s.activation(
                             out=dst_tm[:, 1024 * s_ + 128 * ch0: 1024 * s_ + 128 * ch0 + 512],
                             in_=bank, func=AF.Copy),
                         writes=[dst_name, dst_cn[b]], waits=[tokt], cost=700.0)
            psf_free[bi] = tok_e

    def tree_reduce(src_ap, src_name, scratch, scratch_name, G, H, out_ap, out_name,
                    src_extra_reads=()):
        h = H
        cur, cur_name = src_ap, src_name
        def v3(ap, g, hh):
            return ap[:, 0:g * hh].rearrange("p (g h) -> p g h", h=hh)
        while h > 1:
            nh = h // 2
            dst, dst_name = (scratch, scratch_name) if h == H else (cur, cur_name)
            if nh == 1:
                dst_ap = out_ap.rearrange("p (g o) -> p g o", o=1)
                dn = out_name
            else:
                dst_ap = v3(dst, G, nh)
                dn = dst_name
            i0 = v3(cur, G, h)[:, :, 0:nh]
            i1 = v3(cur, G, h)[:, :, nh:h]
            extra = list(src_extra_reads) if h == H else []
            B.op("vector", lambda v, o=dst_ap, a=i0, b=i1: v.tensor_tensor(
                out=o, in0=a, in1=b, op=ALU.add),
                reads=[cur_name] + extra, writes=[dn],
                cost=(58.0 + G * nh) / 0.96 + 250.0)
            cur, cur_name = dst, dst_name
            h = nh

    def ln_stats(tm_buf, tm_name, sq_buf, sq_name, tr_buf, tr_name, par):
        st = statbufs[par]
        p = str(par)
        sx = st[:, 0:SUB]; sxx = st[:, SUB:2 * SUB]
        mean = st[:, 2 * SUB:3 * SUB]; istd = st[:, 3 * SUB:4 * SUB]
        tv1 = st[:, 4 * SUB:5 * SUB]; tv2 = st[:, 5 * SUB:6 * SUB]
        B.op("scalar", lambda s, o=sq_buf, i=tm_buf: s.activation(
            out=o, in_=i, func=AF.Square),
            reads=[tm_name],
            writes=[sq_name] + [f"{sq_name}@{j}" for j in range(NCH)], cost=3700.0)
        tree_reduce(tm_buf, tm_name, tr_buf, tr_name, SUB, 1024, sx, "sx" + p)
        tree_reduce(sq_buf, sq_name, sq_buf, sq_name, SUB, 1024, sxx, "sxx" + p,
                    src_extra_reads=[f"{sq_name}@{j}" for j in range(NCH)])
        B.op("vector", lambda v, sx=sx, mean=mean: v.tensor_scalar(
            out=mean, in0=sx, scalar1=c_i1024, scalar2=None, op0=ALU.mult),
            reads=["sx" + p, "consts"], writes=["mean" + p], cost=150.0)
        B.op("vector", lambda v, mean=mean, tv1=tv1: v.tensor_tensor(
            out=tv1, in0=mean, in1=mean, op=ALU.mult),
            reads=["mean" + p], writes=["tv1" + p], cost=150.0)
        B.op("vector", lambda v, sxx=sxx, tv1=tv1, tv2=tv2: v.scalar_tensor_tensor(
            out=tv2, in0=sxx, scalar=c_i1024, in1=tv1, op0=ALU.mult, op1=ALU.subtract),
            reads=["sxx" + p, "tv1" + p, "consts"], writes=["tv2" + p], cost=150.0)
        B.op("vector", lambda v, tv2=tv2: v.tensor_scalar(
            out=tv2, in0=tv2, scalar1=c_eps5, scalar2=None, op0=ALU.add),
            reads=["tv2" + p, "consts"], writes=["tv2" + p], cost=150.0)
        B.op("vector", lambda v, tv1=tv1, tv2=tv2: v.tensor_scalar(
            out=tv1, in0=tv2, scalar1=0.5, scalar2=0.5, op0=ALU.mult, op1=ALU.add),
            reads=["tv2" + p], writes=["tv1" + p], cost=150.0)
        for _ in range(2):
            B.op("vector", lambda v, tv1=tv1, istd=istd: v.reciprocal(out=istd, in_=tv1),
                 reads=["tv1" + p], writes=["istd" + p], cost=150.0)
            B.op("vector", lambda v, tv2=tv2, istd=istd: v.tensor_tensor(
                out=istd, in0=tv2, in1=istd, op=ALU.mult),
                reads=["tv2" + p, "istd" + p], writes=["istd" + p], cost=150.0)
            B.op("vector", lambda v, tv1=tv1, istd=istd: v.tensor_tensor(
                out=istd, in0=tv1, in1=istd, op=ALU.add),
                reads=["tv1" + p, "istd" + p], writes=["istd" + p], cost=150.0)
            B.op("vector", lambda v, istd=istd, tv1=tv1: v.tensor_scalar(
                out=tv1, in0=istd, scalar1=0.5, scalar2=None, op0=ALU.mult),
                reads=["istd" + p], writes=["tv1" + p], cost=150.0)
        B.op("vector", lambda v, tv1=tv1, istd=istd: v.reciprocal(out=istd, in_=tv1),
             reads=["tv1" + p], writes=["istd" + p], cost=150.0)

    def normalize(tm_buf, tm_name, u_tm, u_name, par):
        st = statbufs[par]
        p = str(par)
        mean = st[:, 2 * SUB:3 * SUB]; istd = st[:, 3 * SUB:4 * SUB]
        for s_ in range(SUB):
            B.op("vector", lambda v, s_=s_, u=u_tm, i=tm_buf, mean=mean, istd=istd: v.tensor_scalar(
                out=u[:, 1024 * s_: 1024 * s_ + 1024],
                in0=i[:, 1024 * s_: 1024 * s_ + 1024],
                scalar1=mean[:, s_: s_ + 1], scalar2=istd[:, s_: s_ + 1],
                op0=ALU.subtract, op1=ALU.mult),
                reads=[tm_name, "mean" + p, "istd" + p], writes=[u_name], cost=1150.0)

    def cn(base):
        return [f"{base}@{j}" for j in range(NCH)]

    ufms = [(ufm0, "ufm0"), (ufm1, "ufm1")]
    utms = [(u_tm0, "u_tm0"), (u_tm1, "u_tm0")]
    t1s = [(t1a, "t1a"), (t1b, "t1b")]
    xys = [(xy0, "xy0"), (xy1, "xy1")]

    def p1_A(idx):
        xy, xyn = xys[idx % 2]
        t1, t1n = t1s[idx % 2]
        B.op("sync", lambda s, idx=idx, xy=xy: s.dma_start(
            out=xy.rearrange("p (sb d) -> p sb d", d=D),
            in_=x_d[idx * TILE:(idx + 1) * TILE, :].rearrange("(sb p) d -> p sb d", p=128)),
            writes=[xyn], sem=f"sIX{idx % 2}", dma=True, cost=7000.0)
        tm_to_fm_f32(xy, xyn, Y, cn("Y"))
        ln_stats(xy, xyn, t1, t1n, kk, "kk", idx % 2)
        normalize(xy, xyn, *utms[idx % 2], idx % 2)

    def p1_B(idx):
        ufm, ufm_name = ufms[idx % 2]
        u_tm, u_name = utms[idx % 2]
        t1, t1n = t1s[idx % 2]
        tm_to_fm_bf(u_tm, u_name, ufm, ufm_name)
        projection(Mb, "Mb", ufm, ufm_name,
                   lambda s, mb, bank, t1=t1: s.activation(
                       out=t1[:, 512 * mb: 512 * mb + 512], in_=bank,
                       func=AF.Tanh, bias=bias["bfn"][:, mb: mb + 1], scale=1.0),
                   [t1n], extra_reads=["bias"], chunk_names=cn(t1n))
        for mb in range(NCH):
            B.op("vector", lambda v, t1=t1, mb=mb: v.scalar_tensor_tensor(
                out=Y[:, 512 * mb: 512 * mb + 512],
                in0=t1[:, 512 * mb: 512 * mb + 512], scalar=c_half,
                in1=Y[:, 512 * mb: 512 * mb + 512], op0=ALU.mult, op1=ALU.add),
                reads=[cn(t1n)[mb], f"Y@{mb}", "consts"], writes=[f"Y@{mb}"], cost=600.0)
        B.op("sync", lambda s, idx=idx: s.dma_start(out=y1sp[idx], in_=Y),
             reads=cn("Y"), sem="sSY", dma=True, cost=7000.0)
        fm_to_tm(Y, cn("Y"), vv, "vv")
        ln_stats(vv, "vv", t1, t1n, kk, "kk", idx % 2)
        normalize(vv, "vv", u_tm, u_name, idx % 2)
        tm_to_fm_bf(u_tm, u_name, ufm, ufm_name)
        B.op("sync", lambda s, idx=idx, u=ufm: s.dma_start(out=u2sp[idx], in_=u),
             reads=[ufm_name], sem=f"sSU{idx % 2}", dma=True, cost=4000.0)

    def p1_C(idx):
        ufm, ufm_name = ufms[idx % 2]
        projection(WA, "WA", ufm, ufm_name,
                   lambda s, mb, bank: s.activation(
                       out=zk[:, 512 * mb: 512 * mb + 512], in_=bank,
                       func=AF.Identity, bias=bias["bk"][:, mb: mb + 1], scale=1.0),
                   ["zk"], extra_reads=["bias"])
        projection(WB, "WB", ufm, ufm_name,
                   lambda s, mb, bank: s.activation(
                       out=vv[:, 512 * mb: 512 * mb + 512], in_=bank, func=AF.Copy),
                   ["vv"])

    def p1_D(idx):
        jt = idx % TPS
        seq = idx // TPS
        B.op("vector", lambda v: v.tensor_scalar(
            out=kk, in0=zk, scalar1=0.0, scalar2=None, op0=ALU.min),
            reads=["zk"], writes=["kk"], cost=2200.0)
        B.op("scalar", lambda s: s.activation(out=kk, in_=kk, func=AF.Exp),
             reads=["kk"], writes=["kk"], cost=3700.0)
        B.op("vector", lambda v: v.scalar_tensor_tensor(
            out=kk, in0=zk, scalar=0.0, in1=kk, op0=ALU.max, op1=ALU.add),
            reads=["zk", "kk"], writes=["kk"], cost=4300.0)
        B.op("vector", lambda v: v.scalar_tensor_tensor(
            out=zk, in0=kk, scalar=1.0, in1=vv, op0=ALU.mult, op1=ALU.mult),
            reads=["kk", "vv"], writes=["zk"], cost=4300.0)
        tree_reduce(kk, "kk", kk, "kk", NCH, 512,
                    KsumP[:, NCH * jt: NCH * jt + NCH], "KsumP")
        tree_reduce(zk, "zk", zk, "zk", NCH, 512,
                    KVP[:, NCH * jt: NCH * jt + NCH], "KVP")
        if jt == TPS - 1:
            def comb(v, dst, src):
                v.tensor_tensor(out=tk1, in0=src[:, 0:NCH], in1=src[:, NCH:2 * NCH], op=ALU.add)
                v.drain()
                v.tensor_tensor(out=tk1, in0=tk1, in1=src[:, 2 * NCH:3 * NCH], op=ALU.add)
                v.drain()
                return v.tensor_tensor(out=dst[:, NCH * seq: NCH * seq + NCH],
                                       in0=tk1, in1=src[:, 3 * NCH:4 * NCH], op=ALU.add)
            B.op("vector", lambda v, f=comb: f(v, Ksumf, KsumP),
                 reads=["KsumP"], writes=["Ksumf", "tk1"])
            B.op("vector", lambda v, f=comb: f(v, KVf, KVP),
                 reads=["KVP"], writes=["KVf", "tk1"])
            B.op("vector", lambda v, seq=seq: v.tensor_tensor(
                out=tk1, in0=bias["bv"], in1=Ksumf[:, NCH * seq: NCH * seq + NCH],
                op=ALU.mult),
                reads=["bias", "Ksumf"], writes=["tk1"])
            B.op("vector", lambda v, seq=seq: v.tensor_tensor(
                out=KVf[:, NCH * seq: NCH * seq + NCH],
                in0=KVf[:, NCH * seq: NCH * seq + NCH], in1=tk1, op=ALU.add),
                reads=["tk1", "KVf"], writes=["KVf"])

    p1_A(0)
    p1_B(0)
    for k in range(1, NT):
        p1_A(k)
        p1_C(k - 1)
        p1_D(k - 1)
        p1_B(k)
    p1_C(NT - 1)
    p1_D(NT - 1)

    wload(WA, wq_d, "WA")
    wload(WB, wo_d, "WB")
    tok_all_spills_list = [i for i, o in enumerate(B.ops)
                           if o["sem"] in ("sSY", "sSU0", "sSU1")]

    def p2_load_u2(idx, extra=()):
        ufm, ufm_name = ufms[idx % 2]
        B.op("sync", lambda s, idx=idx, u=ufm: s.dma_start(out=u, in_=u2sp[idx]),
             writes=[ufm_name], waits=list(extra), sem=f"sLU{idx % 2}", dma=True, cost=4000.0)

    def p2_load_y1(idx, extra=()):
        B.op("sync", lambda s, idx=idx: s.dma_start(out=Y, in_=y1sp[idx]),
             writes=cn("Y"), waits=list(extra), sem="sLY", dma=True, cost=7000.0)

    def p2_Q(idx):
        ufm, ufm_name = ufms[idx % 2]
        projection(WA, "WA", ufm, ufm_name,
                   lambda s, mb, bank: s.activation(
                       out=zk[:, 512 * mb: 512 * mb + 512], in_=bank,
                       func=AF.Identity, bias=bias["bq"][:, mb: mb + 1], scale=1.0),
                   ["zk"], extra_reads=["bias"])

    def p2_R(idx):
        seq = idx // TPS
        t1, t1n = t1s[idx % 2]
        B.op("vector", lambda v: v.tensor_scalar(
            out=kk, in0=zk, scalar1=0.0, scalar2=None, op0=ALU.min),
            reads=["zk"], writes=["kk"], cost=2200.0)
        B.op("scalar", lambda s: s.activation(out=kk, in_=kk, func=AF.Exp),
             reads=["kk"], writes=["kk"], cost=3700.0)
        B.op("vector", lambda v: v.scalar_tensor_tensor(
            out=kk, in0=zk, scalar=0.0, in1=kk, op0=ALU.max, op1=ALU.add),
            reads=["zk", "kk"], writes=["kk"], cost=4300.0)
        for c in range(NCH):
            B.op("vector", lambda v, c=c, seq=seq: v.tensor_scalar(
                out=vv[:, 512 * c: 512 * c + 512],
                in0=kk[:, 512 * c: 512 * c + 512],
                scalar1=Ksumf[:, NCH * seq + c: NCH * seq + c + 1],
                scalar2=c_eps6, op0=ALU.mult, op1=ALU.add),
                reads=["kk", "Ksumf", "consts"], writes=["vv"], cost=600.0)
        B.op("scalar", lambda s, t1=t1: s.activation(out=t1, in_=vv, func=AF.Ln),
             reads=["vv"], writes=[t1n] + cn(t1n), cost=3700.0)
        for c in range(NCH):
            B.op("vector", lambda v, c=c, seq=seq: v.tensor_scalar(
                out=kk[:, 512 * c: 512 * c + 512],
                in0=kk[:, 512 * c: 512 * c + 512],
                scalar1=KVf[:, NCH * seq + c: NCH * seq + c + 1],
                scalar2=None, op0=ALU.mult),
                reads=["kk", "KVf"], writes=["kk"], cost=600.0)
        B.op("scalar", lambda s, t1=t1: s.activation(out=t1, in_=t1, func=AF.Exp, scale=-1.0),
             reads=[t1n], writes=[t1n] + cn(t1n), cost=3700.0)
        u_tm, u_name = utms[idx % 2]
        B.op("vector", lambda v, u=u_tm, t1=t1: v.tensor_tensor(
            out=u, in0=kk, in1=t1, op=ALU.mult),
            reads=["kk", t1n] + cn(t1n), writes=[u_name], cost=4300.0)

    def p2_W(idx):
        u_tm, u_name = utms[idx % 2]
        t1, t1n = t1s[idx % 2]
        projection(WB, "WB", u_tm, u_name,
                   lambda s, mb, bank, t1=t1: s.activation(
                       out=t1[:, 512 * mb: 512 * mb + 512], in_=bank,
                       func=AF.Gelu_apprx_tanh, bias=bias["bo"][:, mb: mb + 1], scale=1.0),
                   [t1n], extra_reads=["bias"], chunk_names=cn(t1n))
        for mb in range(NCH):
            B.op("vector", lambda v, t1=t1, mb=mb: v.tensor_tensor(
                out=Y[:, 512 * mb: 512 * mb + 512],
                in0=Y[:, 512 * mb: 512 * mb + 512],
                in1=t1[:, 512 * mb: 512 * mb + 512], op=ALU.add),
                reads=[f"Y@{mb}", cn(t1n)[mb]], writes=[f"Y@{mb}"], cost=600.0)

    def p2_L3(idx):
        xy, xyn = xys[idx % 2]
        t1, t1n = t1s[idx % 2]
        fm_to_tm(Y, cn("Y"), xy, xyn)
        ln_stats(xy, xyn, t1, t1n, vv, "vv", idx % 2)
        normalize(xy, xyn, *utms[idx % 2], idx % 2)

    def p2_F(idx):
        ufm, ufm_name = ufms[idx % 2]
        u_tm, u_name = utms[idx % 2]
        t1, t1n = t1s[idx % 2]
        xy, xyn = xys[idx % 2]
        tm_to_fm_bf(u_tm, u_name, ufm, ufm_name)
        projection(Mb, "Mb", ufm, ufm_name,
                   lambda s, mb, bank, t1=t1: s.activation(
                       out=t1[:, 512 * mb: 512 * mb + 512], in_=bank,
                       func=AF.Tanh, bias=bias["bfn"][:, mb: mb + 1], scale=1.0),
                   [t1n], extra_reads=["bias"], chunk_names=cn(t1n))
        for mb in range(NCH):
            B.op("vector", lambda v, t1=t1, mb=mb: v.scalar_tensor_tensor(
                out=vv[:, 512 * mb: 512 * mb + 512],
                in0=t1[:, 512 * mb: 512 * mb + 512], scalar=c_half,
                in1=Y[:, 512 * mb: 512 * mb + 512], op0=ALU.mult, op1=ALU.add),
                reads=[cn(t1n)[mb], f"Y@{mb}", "consts", "vv"], writes=[cn("vvo")[mb]],
                cost=600.0)
        fm_to_tm(vv, cn("vvo"), xy, xyn, extra_reads=["vv"])
        B.op("sync", lambda s, idx=idx, xy=xy: s.dma_start(
            out=out_d[idx * TILE:(idx + 1) * TILE, :].rearrange("(sb p) d -> p sb d", p=128),
            in_=xy.rearrange("p (sb d) -> p sb d", d=D)),
            reads=[xyn], sem=f"sOU{idx % 2}", dma=True, cost=7000.0)

    p2_load_u2(0, extra=tok_all_spills_list)
    p2_load_y1(0, extra=tok_all_spills_list)
    p2_Q(0)
    for k in range(NT):
        if k > 0:
            p2_F(k - 1)
            p2_load_y1(k)
        if k + 1 < NT:
            p2_load_u2(k + 1)
        p2_R(k)
        p2_W(k)
        p2_L3(k)
        if k + 1 < NT:
            p2_Q(k + 1)
    p2_F(NT - 1)

    all_stores = [i for i, o in enumerate(B.ops)
                  if o["sem"] in ("sOU0", "sOU1", "sSY", "sSU0", "sSU1")]
    B.op("sync", lambda s: s.nop(), waits=all_stores, cost=10.0)
    B.emit()
    return nc


def kernel(L0_attn_x, L0_attn_Wq, L0_attn_bq, L0_attn_Wk, L0_attn_bk,
           L0_attn_Wv, L0_attn_bv, L0_attn_Wo, L0_attn_bo,
           L0_ffn_weight, L0_ffn_aux_weight1, L0_ffn_aux_weight2, L0_ffn_bias):
    import ml_dtypes
    x = np.ascontiguousarray(np.asarray(L0_attn_x, dtype=np.float32))
    Wq = np.ascontiguousarray(np.asarray(L0_attn_Wq, np.float32))
    Wk = np.ascontiguousarray(np.asarray(L0_attn_Wk, np.float32))
    Wv = np.ascontiguousarray(np.asarray(L0_attn_Wv, np.float32))
    Wo = np.ascontiguousarray(np.asarray(L0_attn_Wo, np.float32))
    W = np.asarray(L0_ffn_weight, np.float32)
    A1 = np.ascontiguousarray(np.asarray(L0_ffn_aux_weight1, np.float32))
    A2 = np.ascontiguousarray(np.asarray(L0_ffn_aux_weight2, np.float32))
    Wt = np.ascontiguousarray(W.T)

    def bias_fm(b):
        return np.ascontiguousarray(np.asarray(b, np.float32).reshape(NCH, 128).T)

    ident = np.eye(128, dtype=np.float32)
    identbf = ident.astype(ml_dtypes.bfloat16)

    nc = build_program()
    in_maps = []
    for ci in range(NCORES):
        xs = x[ci * SEQ_PER_CORE:(ci + 1) * SEQ_PER_CORE].reshape(SEQ_PER_CORE * S_LEN, D)
        in_maps.append({
            "x": np.ascontiguousarray(xs),
            "Wt": Wt, "A1": A1, "A2": A2,
            "Wq": Wq, "Wk": Wk, "Wv": Wv, "Wo": Wo,
            "bq": bias_fm(L0_attn_bq), "bk": bias_fm(L0_attn_bk),
            "bv": bias_fm(L0_attn_bv), "bo": bias_fm(L0_attn_bo),
            "bfn": bias_fm(L0_ffn_bias),
            "I32": ident, "Ibf": identbf,
        })
    res = run_bass_kernel_spmd(nc, in_maps, core_ids=list(range(NCORES)))
    out = np.concatenate([r["out"].reshape(SEQ_PER_CORE, S_LEN, D)
                          for r in res.results], axis=0)
    return out
